# version 08
# baseline (speedup 1.0000x reference)
import numpy as np
import ml_dtypes
import concourse.bass as bass
import concourse.mybir as mybir
from concourse.bass_utils import run_bass_kernel_spmd

F32 = mybir.dt.float32
BF16 = mybir.dt.bfloat16
I32 = mybir.dt.int32
AF = mybir.ActivationFunctionType
ALU = mybir.AluOpType
AX = mybir.AxisListType.X
bf16 = ml_dtypes.bfloat16

DM, DI, DS, NH, HD, DCONV = 1024, 2048, 128, 16, 128, 4
DXBC = DI + 2 * DS
DPROJ = 2 * DI + 2 * DS + NH
NCT = DXBC // 128
EPS = 1e-5
Q = 128
TT = 512
NCH = TT // Q
BIG = 1e5
MAGIC = 777


class Syn:
    ENGS = ("pe", "act", "dve", "pool", "sq", "gq", "g2")

    def __init__(self, nc, ctx):
        self.sem = {e: ctx.enter_context(nc.semaphore("sy_" + e)) for e in self.ENGS}
        self.n = {e: 0 for e in self.ENGS}
        self.waited = {e: {p: 0 for p in self.ENGS} for e in self.ENGS}

    def done(self, eng, instr, k=1):
        instr.then_inc(self.sem[eng], k)
        self.n[eng] += k
        return (eng, self.n[eng])

    def tok(self, eng):
        return (eng, self.n[eng])

    def wait(self, ceng, c, *tokens):
        phys = {"gq": "pool", "g2": "pool", "sq": "sq"}
        me = phys.get(ceng, ceng)
        for tokn in tokens:
            if tokn is None:
                continue
            prod, val = tokn
            if phys.get(prod, prod) == me or val <= 0:
                continue
            if self.waited[ceng][prod] < val:
                c.wait_ge(self.sem[prod], val)
                self.waited[ceng][prod] = val


def build(ntile=8, dbg=False, bar2=False, bar3=False):
    tcore = ntile * TT
    DBG = "ExternalOutput" if dbg else "Internal"
    from contextlib import ExitStack
    nc = bass.Bass("TRN2", target_bir_lowering=False, detect_race_conditions=False)

    def din(name, shape, dt=F32):
        return nc.dram_tensor(name, shape, dt, kind="ExternalInput").ap()

    xs = din("xs", [3 + tcore, DM])
    winr = din("winr", [128, 8, DPROJ])
    woutr = din("woutr", [128, 16, DM])
    w1r = din("w1r", [9, 128, 8, 512])
    w2r = din("w2r", [2, 128, 9, 4, 512])
    wrr = din("wrr", [128, 8, 8])
    convw = din("convw", [128, NCT, DCONV])
    convb = din("convb", [128, NCT])
    lnw = din("lnw", [128, 8])
    lnb = din("lnb", [128, 8])
    rmsc = din("rmsc", [128, 16])
    alog = din("alog", [16, 1])
    dtbb = din("dtbb", [128, 16])
    ddiagf = din("ddiagf", [128, 16 * 128])
    gates = din("gates", [128, 1])
    role = din("role", [1, 2], I32)
    identb = din("identb", [128, 128], BF16)
    identf = din("identf", [128, 128], F32)
    ones2b = din("ones2b", [2, 128], BF16)
    ones1f = din("ones1f", [1, 128], F32)
    ones16f = din("ones16f", [16, 128], F32)
    maskTb = din("maskTb", [128, 128], BF16)
    selb = din("selb", [8, 8 * 128], BF16)
    fcst = din("fcst", [128, 6])

    out_h = nc.dram_tensor("out", [tcore, DM], F32, kind="ExternalOutput").ap()

    sp_y = nc.dram_tensor("sp_y", [ntile, NCH, 128, DI], F32, kind=DBG).ap()
    sp_z = nc.dram_tensor("sp_z", [ntile, NCH, 128, DI], BF16, kind=DBG).ap()
    sp_xnT = nc.dram_tensor("sp_xnT", [ntile, NCH, 128, DM], F32, kind=DBG).ap()
    sp_xnF = nc.dram_tensor("sp_xnF", [128, 8, tcore], BF16, kind=DBG).ap()
    sp_C = nc.dram_tensor("sp_C", [128, tcore], BF16, kind=DBG).ap()
    sp_eg = nc.dram_tensor("sp_eg", [ntile, 128, NCH * 16], F32, kind=DBG).ap()
    sp_wo = nc.dram_tensor("sp_wo", [128, 16 * DM], BF16, kind=DBG).ap()
    if dbg:
        sp_s = nc.dram_tensor("sp_s", [ntile, NCH, 128, DM], F32, kind="ExternalOutput").ap()
        sp_g = nc.dram_tensor("sp_g", [ntile, 2, NCH, 128, 512], F32, kind="ExternalOutput").ap()
        sp_yc = nc.dram_tensor("sp_yc", [ntile, NCH, 128, DI], F32, kind="ExternalOutput").ap()
        sp_x2 = nc.dram_tensor("sp_x2", [ntile, NCH, 128, DM], F32, kind="ExternalOutput").ap()
        sp_r = nc.dram_tensor("sp_r", [ntile, NCH, 128, 12], F32, kind="ExternalOutput").ap()
        sp_sz = nc.dram_tensor("sp_sz", [ntile, NCH, 128, DI], BF16, kind="ExternalOutput").ap()
        sp_zp = nc.dram_tensor("sp_zp", [ntile, NCH, 128, DI], BF16, kind="ExternalOutput").ap()
        sp_yg = nc.dram_tensor("sp_yg", [ntile, NCH, 128, DI], F32, kind="ExternalOutput").ap()
        sp_y3 = nc.dram_tensor("sp_y3", [ntile, NCH, 128, 16 * 128], BF16, kind="ExternalOutput").ap()
    sh_pay = nc.dram_tensor("sh_pay", [2, 128, NH * HD], F32, kind="Internal",
                            addr_space="Shared").ap()
    sh_flag = nc.dram_tensor("sh_flag", [2, 1, 2], I32, kind="Internal",
                             addr_space="Shared").ap()

    from concourse.hw_specs import get_activation_tables
    _tabs = list(get_activation_tables(nc.m.arch).keys())
    NLE_ID = _tabs.index("natural_log_exp_and_others")
    SIL_ID = _tabs.index("silu_and_others")

    def tload(s, sid_):
        s.add_instruction(mybir.InstLoadActFuncSet(
            name=nc.get_next_instruction_name(), act_func_set_id=sid_,
            ins=[], outs=[]))

    ctx = ExitStack()
    S_ = lambda n, sh, dt: ctx.enter_context(nc.sbuf_tensor(n, sh, dt))
    P_ = lambda n, sh, dt: ctx.enter_context(nc.psum_tensor(n, sh, dt))
    sy = Syn(nc, ctx)
    W, D_, Tk = sy.wait, sy.done, sy.tok
    TE, AE, VE, GE, SE, GQ, G2 = "pe", "act", "dve", "pool", "sq", "gq", "g2"

    scw = S_("scw", [128, NCT * DCONV], F32)
    scb = S_("scb", [128, NCT], F32)
    slnw = S_("slnw", [128, 8], F32)
    slnb = S_("slnb", [128, 8], F32)
    srms = S_("srms", [128, 16], F32)
    sacol = S_("sacol", [16, 1], F32)
    sdtb = S_("sdtb", [128, 16], F32)
    sgate = S_("sgate", [128, 1], F32)
    srole = S_("srole", [1, 2], I32)
    sflag = S_("sflag", [1, 2], I32)
    sid = S_("sid", [128, 128], BF16)
    sidf = S_("sidf", [128, 128], F32)
    so2 = S_("so2", [2, 128], BF16)
    so1f = S_("so1f", [1, 128], F32)
    so16f = S_("so16f", [16, 128], F32)
    smk = S_("smk", [128, 128], BF16)
    ssel = S_("ssel", [8, 8 * 128], BF16)
    prefr = [S_("prefr0", [1, 16], F32), S_("prefr1", [1, 16], F32)]
    sfc = S_("sfc", [128, 6], F32)
    warmb = S_("warmb", [1, 8], F32)
    for _j, _v in enumerate([EPS, 1.0, -1.0, -0.5, 0.5, 0.0]):
        nc.const_aps.register(F32, _v, sfc[:, _j:_j + 1])

    p1 = ExitStack()
    S1 = lambda n, sh, dt: p1.enter_context(nc.sbuf_tensor(n, sh, dt))
    winb = S1("winb", [128, 8 * DPROJ], BF16)
    sddiag = S1("sddiag", [128, 16 * 128], BF16)
    sS = S1("sS", [128, NH * HD], F32)
    sSb = S1("sSb", [128, NH * HD], BF16)

    pA = P_("pA", [128, 512], F32)
    pB = P_("pB", [128, 512], F32)
    pT = [P_("pT0", [128, 1024], BF16), P_("pT1", [128, 1024], BF16)]
    pS = [P_("pS0", [128, 512], F32), P_("pS1", [128, 512], F32)]
    pY = [P_("pY0", [128, 512], F32), P_("pY1", [128, 512], F32)]

    blkctx = nc.Block()
    block = blkctx.__enter__()

    @block.sync
    def _(sq):
        for dst, src in [(scw[:], convw.rearrange("p a b -> p (a b)")),
                         (scb[:], convb), (slnw[:], lnw), (slnb[:], lnb),
                         (srms[:], rmsc), (sacol[:], alog), (sdtb[:], dtbb),
                         (sgate[:], gates), (srole[:], role),
                         (sidf[:], identf), (so1f[:], ones1f), (so16f[:], ones16f), (sfc[:], fcst)]:
            D_(SE, sq.dma_start(out=dst, in_=src), 16)

    @block.gpsimd
    def _(gq):
        for dst, src in [(sid[:], identb), (so2[:], ones2b), (smk[:], maskTb),
                         (ssel[:], selb), (sddiag[:], ddiagf),
                         (winb[:], winr.rearrange("p a b -> p (a b)"))]:
            D_(GQ, gq.dma_start(out=dst, in_=src), 16)
        gq.memset(sS[:], 0.0)
        gq.memset(sSb[:], 0.0)
        gq.memset(prefr[0][:], 0.0)
        g = gq.memset(prefr[1][:], 0.0)
        D_(GE, g)

    @block.scalar
    def _(a):
        tload(a, NLE_ID)
        W(AE, a, Tk(SE))
        a.activation(sacol[:], sacol[:], AF.Exp)
        D_(AE, a.activation(sacol[:], sacol[:], AF.Copy, scale=-1.0))

    setup_done = {e: Tk(e) for e in Syn.ENGS}

    xb = [S1("xb0", [128, DM], F32), S1("xb1", [128, DM], F32)]
    lnt = S1("lnt", [128, DM], F32)
    stat = S1("stat", [128, 12], F32)
    xnT = S1("xnT", [128, DM], F32)
    ln2 = S1("ln2", [128, DM], F32)
    xnbf = S1("xnbf", [128, DM], BF16)
    ln2bf = S1("ln2bf", [128, DM], BF16)
    xnFc = S1("xnFc", [128, 8 * 128], BF16)
    xn2F = S1("xn2F", [128, 8 * TT], BF16)
    zc = [S1("zc0", [128, DI], BF16), S1("zc1", [128, DI], BF16)]
    dtT = S1("dtT", [128, NCH * 16], F32)
    vsm = S1("vsm", [128, 16], F32)
    vsm2 = S1("vsm2", [128, 16], F32)
    raw = S1("raw", [128, NCT * (TT + 3)], BF16)
    hist = S1("hist", [128, NCT * 3], BF16)
    xbcs = S1("xbcs", [128, NCT * TT], BF16)
    cvt = [S1("cvt0", [128, TT], F32), S1("cvt1", [128, TT], F32)]
    r16 = S1("r16", [16, 128], F32)
    dtrows = S1("dtrows", [16, 128], F32)
    lrows = S1("lrows", [16, 128], F32)
    hirows = S1("hirows", [16, TT], BF16)
    lorows = S1("lorows", [16, TT], BF16)
    hl2 = S1("hl2", [2, NH * TT], BF16)
    lT = S1("lT", [128, NCH * 16], F32)
    nlT = S1("nlT", [128, NCH * 16], F32)
    eTc = S1("eTc", [128, NCH * 16], F32)
    lendr = S1("lendr", [1, NCH * 16], F32)
    eendT = S1("eendT", [128, NCH * 16], F32)
    endfT = S1("endfT", [128, NCH * 16], F32)
    egT = S1("egT", [128, NCH * 16], F32)
    xtc = [S1("xtc0", [128, 128], BF16), S1("xtc1", [128, 128], BF16)]
    btc = S1("btc", [128, 128], BF16)
    btS = [S1("btS0", [128, 128], BF16), S1("btS1", [128, 128], BF16)]
    sDm = [S1("sD0", [128, 128], BF16), S1("sD1", [128, 128], BF16)]
    ssc = [S1("ssc0", [128, 128], BF16), S1("ssc1", [128, 128], BF16)]
    sy2 = [S1("sy20", [128, 128], F32), S1("sy21", [128, 128], F32)]
    sdS = [S1("sdS0", [128, 128], F32), S1("sdS1", [128, 128], F32)]
    ylc = [S1("yl0", [128, DI // 2], F32), S1("yl1", [128, DI // 2], F32)]
    haloT = S1("haloT", [128, 8 * 4], BF16)

    def selfwait(v, eng=None):
        e = eng or VE
        v.wait_ge(sy.sem[e], sy.n[e])

    def hardbar():
        _all = [Tk(e) for e in Syn.ENGS if e != "gq"]
        @block.tensor
        def _(tt):
            W(TE, tt, *_all)
        @block.scalar
        def _(a):
            W(AE, a, *_all)
        @block.vector
        def _(v):
            W(VE, v, *_all)
        @block.gpsimd
        def _(g):
            W(GE, g, *_all)
        @block.sync
        def _(sq):
            W(SE, sq, *_all)

    def warm(func):
        sid_ = SIL_ID if func == AF.Silu else NLE_ID
        @block.scalar
        def _(a):
            tload(a, sid_)

    def emit_ln(src, dst_f32, dst_bf, p=128):
        @block.vector
        def _(v):
            D_(VE, v.reduce_sum(stat[0:p, 0:1], src, axis=AX))
            v.tensor_mul(lnt[0:p, :], src, src)
            selfwait(v)
            D_(VE, v.reduce_sum(stat[0:p, 1:2], lnt[0:p, :], axis=AX))
            selfwait(v)
            D_(VE, v.tensor_scalar(stat[0:p, 2:3], stat[0:p, 0:1], 1.0 / DM, None, ALU.mult))
            selfwait(v)
            D_(VE, v.tensor_mul(stat[0:p, 3:4], stat[0:p, 2:3], stat[0:p, 2:3]))
            selfwait(v)
            i = v.scalar_tensor_tensor(stat[0:p, 5:6], stat[0:p, 1:2], 1.0 / DM,
                                       stat[0:p, 3:4], ALU.mult, ALU.subtract)
            D_(VE, i)

        @block.scalar
        def _(a):
            W(AE, a, Tk(VE))
            D_(AE, a.activation(stat[0:p, 6:7], stat[0:p, 5:6], AF.Ln, bias=EPS))
            a.wait_ge(sy.sem[AE], sy.n[AE])
            D_(AE, a.activation(stat[0:p, 7:8], stat[0:p, 6:7], AF.Exp, scale=-0.5))

        @block.vector
        def _(v):
            W(VE, v, Tk(AE))
            D_(VE, v.tensor_scalar(dst_f32, src, stat[0:p, 2:3], stat[0:p, 7:8],
                                   ALU.subtract, ALU.mult))
            selfwait(v)
            D_(VE, v.tensor_copy(dst_bf, dst_f32))
        return Tk(VE)

    @block.sync
    def _(sq):
        W(SE, sq, setup_done[SE])
        D_(SE, sq.dma_start(out=xnT[0:3, :], in_=xs[0:3, :]), 16)

    @block.vector
    def _(v):
        W(VE, v, Tk(SE))
        D_(VE, v.tensor_copy(ln2[0:3, :], xnT[0:3, :]))

    @block.vector
    def _(v):
        selfwait(v)
        v.engine_nop()
    emit_ln(ln2[0:3, :], xnT[0:3, :], xnbf[0:3, :], p=3)
    emit_ln(xnT[0:3, :], ln2[0:3, :], ln2bf[0:3, :], p=3)

    @block.tensor
    def _(t):
        W(TE, t, Tk(VE), setup_done[GQ])
        i = None
        for d in range(8):
            i = t.transpose(pT[0][:, d * 128:d * 128 + 3],
                            ln2bf[0:3, d * 128:(d + 1) * 128], sid[0:3, 0:3])
        D_(TE, i)

    @block.scalar
    def _(a):
        W(AE, a, Tk(TE))
        i = None
        for d in range(8):
            i = a.activation(haloT[:, d * 4:d * 4 + 3], pT[0][:, d * 128:d * 128 + 3],
                             AF.Identity, scale=slnw[:, d:d + 1], bias=slnb[:, d:d + 1])
        D_(AE, i)

    @block.tensor
    def _(t):
        W(TE, t, Tk(AE))
        i = None
        for ct in range(NCT):
            for d in range(8):
                i = t.matmul(pA[:, ct * 4:ct * 4 + 3],
                             winb[:, d * DPROJ + DI + ct * 128:d * DPROJ + DI + (ct + 1) * 128],
                             haloT[:, d * 4:d * 4 + 3], start=(d == 0), stop=(d == 7))
        D_(TE, i)

    @block.vector
    def _(v):
        W(VE, v, Tk(TE))
        i = None
        for ct in range(NCT):
            i = v.tensor_scalar(hist[:, ct * 3:(ct + 1) * 3], pA[:, ct * 4:ct * 4 + 3],
                                sgate[:, 0:1], None, ALU.mult)
        D_(VE, i)
    hist_tok = Tk(VE)

    xtoks = {}
    trtoks = {}
    evac_toks = {}
    tok_xnTsp = {}
    tok_xnFsp = {}
    tok_ln_done = {}
    tok_yspill = {}
    tok_zspill = {}
    tok_supd = [setup_done[GE]] * NH
    tok_supd_pre = [setup_done[GE]] * NH
    toks = {}
    pref_cur = 0

    for t in range(ntile):
        for ck in range(NCH):
            gck = t * NCH + ck
            tr_prev = trtoks.get(gck - 1)

            @block.sync
            def _(sq):
                if gck == 0:
                    xtoks[0] = D_(SE, sq.dma_start(
                        out=xb[0][:],
                        in_=xs[3:3 + 128, :]), 16)
                if gck + 1 < ntile * NCH:
                    W(SE, sq, tok_ln_done.get(gck - 1))
                    xtoks[gck + 1] = D_(SE, sq.dma_start(
                        out=xb[(gck + 1) % 2][:],
                        in_=xs[3 + (gck + 1) * 128:3 + (gck + 2) * 128, :]), 16)

            xsl = xb[gck % 2][:]

            @block.vector
            def _(v):
                W(VE, v, xtoks[gck], tok_xnTsp.get(gck - 1), tr_prev)
                D_(VE, v.engine_nop())
            emit_ln(xsl, xnT[:], xnbf[:])
            t_xn = emit_ln(xnT[:], ln2[:], ln2bf[:])
            tok_ln_done[gck] = t_xn

            @block.sync
            def _(sq):
                W(SE, sq, t_xn)
                tok_xnTsp[gck] = D_(SE, sq.dma_start(out=sp_xnT[t, ck], in_=xnT[:]), 16)

            @block.tensor
            def _(tt):
                W(TE, tt, t_xn, evac_toks.get(gck - 1))
                i = None
                for d in range(8):
                    i = tt.transpose(pT[0][:, d * 128:(d + 1) * 128],
                                     ln2bf[:, d * 128:(d + 1) * 128], sid[:])
                D_(TE, i)
                for d in range(8):
                    i = tt.transpose(pT[1][:, d * 128:(d + 1) * 128],
                                     xnbf[:, d * 128:(d + 1) * 128], sid[:])
                D_(TE, i)
            tr_tok = Tk(TE)
            trtoks[gck] = tr_tok

            @block.scalar
            def _(a):
                W(AE, a, (TE, tr_tok[1] - 1))
                i = None
                for d in range(8):
                    i = a.activation(xn2F[:, d * TT + ck * 128:d * TT + (ck + 1) * 128],
                                     pT[0][:, d * 128:(d + 1) * 128], AF.Identity,
                                     scale=slnw[:, d:d + 1], bias=slnb[:, d:d + 1])
                D_(AE, i)
                W(AE, a, tr_tok, tok_xnFsp.get(gck - 1))
                for d in range(8):
                    i = a.activation(xnFc[:, d * 128:(d + 1) * 128],
                                     pT[1][:, d * 128:(d + 1) * 128], AF.Copy)
                D_(AE, i)
                evac_toks[gck] = Tk(AE)

            @block.sync
            def _(sq):
                W(SE, sq, Tk(AE))
                tok_xnFsp[gck] = D_(SE, sq.dma_start(
                    out=sp_xnF[:, :, t * TT + ck * 128:t * TT + (ck + 1) * 128],
                    in_=xnFc[:].rearrange("p (a b) -> p a b", a=8)), 16)

        xn2F_tok = Tk(AE)

        zprev = None
        dtve_prev = None
        for ck in range(NCH):
            gck = t * NCH + ck
            zpar = ck % 2
            ztoks = []
            for zs in range(4):
                pz = [pA, pB][zs % 2]

                @block.tensor
                def _(tt):
                    W(TE, tt, xn2F_tok)
                    if len(ztoks) >= 2:
                        W(TE, tt, ztoks[-2][1])
                    elif zprev is not None:
                        W(TE, tt, zprev[zs + 2][1])
                        if zs == 0:
                            W(TE, tt, dtve_prev)
                    i = None
                    for d in range(8):
                        i = tt.matmul(pz[:, 0:512],
                                      xn2F[:, d * TT + ck * 128:d * TT + (ck + 1) * 128],
                                      winb[:, d * DPROJ + zs * 512:d * DPROJ + (zs + 1) * 512],
                                      start=(d == 0), stop=(d == 7))
                    ztoks.append([D_(TE, i), None])

                @block.scalar
                def _(a):
                    W(AE, a, ztoks[zs][0])
                    if zs == 0:
                        W(AE, a, tok_zspill.get(gck - 2))
                    i = a.activation(zc[zpar][:, zs * 512:(zs + 1) * 512], pz[:, 0:512],
                                     AF.Copy)
                    ztoks[zs][1] = D_(AE, i)

            @block.tensor
            def _(tt):
                W(TE, tt, ztoks[2][1])
                i = None
                for d in range(8):
                    i = tt.matmul(pA[:, 0:16],
                                  xn2F[:, d * TT + ck * 128:d * TT + (ck + 1) * 128],
                                  winb[:, d * DPROJ + DPROJ - 16:d * DPROJ + DPROJ],
                                  start=(d == 0), stop=(d == 7))
                D_(TE, i)

            @block.vector
            def _(v):
                W(VE, v, Tk(TE), Tk(AE))
                D_(VE, v.tensor_add(vsm[:], pA[:, 0:16], sdtb[:]))
            zprev = ztoks
            dtve_prev = Tk(VE)

            @block.scalar
            def _(a):
                W(AE, a, Tk(VE))
                D_(AE, a.activation(vsm2[:], vsm[:], AF.Abs))
                selfwait(a, AE)
                D_(AE, a.activation(vsm2[:], vsm2[:], AF.Exp, scale=-1.0))
                selfwait(a, AE)
                D_(AE, a.activation(vsm2[:], vsm2[:], AF.Ln, bias=1.0))
                D_(AE, a.activation(lnt[:, 0:16], vsm[:], AF.Relu))

            @block.vector
            def _(v):
                W(VE, v, Tk(AE))
                D_(VE, v.tensor_add(dtT[:, ck * 16:(ck + 1) * 16], vsm2[:], lnt[:, 0:16]))

            @block.sync
            def _(sq):
                W(SE, sq, ztoks[3][1])
                tok_zspill[gck] = D_(SE, sq.dma_start(out=sp_z[t, ck],
                                                      in_=zc[zpar][:]), 16)

        ct_toks = []
        for ct in range(NCT):
            pz = [pA, pB][ct % 2]

            @block.tensor
            def _(tt):
                if ct <= 1:
                    W(TE, tt, Tk(VE), Tk(AE))
                if len(ct_toks) >= 2:
                    W(TE, tt, ct_toks[-2][1])
                i = None
                for d in range(8):
                    i = tt.matmul(pz[:, 0:512],
                                  winb[:, d * DPROJ + DI + ct * 128:d * DPROJ + DI + (ct + 1) * 128],
                                  xn2F[:, d * TT:(d + 1) * TT],
                                  start=(d == 0), stop=(d == 7))
                ct_toks.append([D_(TE, i), None])

            @block.scalar
            def _(a):
                W(AE, a, ct_toks[ct][0])
                i = a.activation(raw[:, ct * (TT + 3) + 3:ct * (TT + 3) + 3 + TT],
                                 pz[:, 0:512], AF.Copy)
                ct_toks[ct][1] = D_(AE, i)

        @block.vector
        def _(v):
            W(VE, v, hist_tok, ct_toks[0][1])
            i = v.tensor_copy(raw[:].rearrange("p (a b) -> p a b", b=TT + 3)[:, :, 0:3],
                              hist[:].rearrange("p (a b) -> p a b", b=3))
            D_(VE, i)
        histcp_tok = Tk(VE)

        warm(AF.Silu)
        conv_toks = []
        for ct in range(NCT):
            rb = ct * (TT + 3)

            cv = cvt[ct % 2]

            @block.vector
            def _(v):
                W(VE, v, ct_toks[ct][1], histcp_tok)
                if ct >= 2:
                    W(VE, v, conv_toks[ct - 2])
                D_(VE, v.tensor_scalar(cv[:], raw[:, rb:rb + TT],
                                       scw[:, ct * 4:ct * 4 + 1], None, ALU.mult))
                for k in range(1, 4):
                    selfwait(v)
                    i = v.scalar_tensor_tensor(cv[:], raw[:, rb + k:rb + k + TT],
                                               scw[:, ct * 4 + k:ct * 4 + k + 1],
                                               cv[:], ALU.mult, ALU.add)
                    D_(VE, i)

            @block.scalar
            def _(a):
                W(AE, a, Tk(VE))
                i = a.activation(xbcs[:, ct * TT:(ct + 1) * TT], cv[:], AF.Silu,
                                 bias=scb[:, ct:ct + 1])
                conv_toks.append(D_(AE, i))

        @block.vector
        def _(v):
            W(VE, v, conv_toks[-1])
            i = v.tensor_copy(hist[:].rearrange("p (a b) -> p a b", b=3),
                              raw[:].rearrange("p (a b) -> p a b", b=TT + 3)[:, :, TT:TT + 3])
            D_(VE, i)
        hist_tok = Tk(VE)

        @block.sync
        def _(sq):
            W(SE, sq, conv_toks[17])
            D_(SE, sq.dma_start(out=sp_C[:, t * TT:(t + 1) * TT],
                                in_=xbcs[:, 17 * TT:18 * TT]), 16)

        warm(AF.Exp)
        for ck in range(NCH):
            par = ck % 2
            c16 = slice(ck * 16, (ck + 1) * 16)

            @block.tensor
            def _(tt):
                W(TE, tt, Tk(VE), Tk(AE))
                i = tt.transpose(pS[par][0:16, 256:384],
                                 dtT[:, ck * 16:(ck + 1) * 16], sidf[:])
                D_(TE, i)

            @block.vector
            def _(v):
                W(VE, v, Tk(TE))
                if ck == 0 and t >= 1:
                    W(VE, v, hl_tok)
                D_(VE, v.tensor_copy(dtrows[:], pS[par][0:16, 256:384]))
                selfwait(v)
                D_(VE, v.tensor_scalar(r16[:], dtrows[:], sacol[:], None, ALU.mult))
                selfwait(v)
                D_(VE, v.tensor_tensor_scan(lrows[:], so16f[:], r16[:], 0.0,
                                            ALU.mult, ALU.add))
                selfwait(v)
                D_(VE, v.tensor_copy(hirows[:, ck * 128:(ck + 1) * 128], lrows[:]))
                selfwait(v)
                D_(VE, v.tensor_sub(r16[:], lrows[:],
                                    hirows[:, ck * 128:(ck + 1) * 128]))
                selfwait(v)
                i = v.tensor_copy(lorows[:, ck * 128:(ck + 1) * 128], r16[:])
                D_(VE, i)

            @block.tensor
            def _(tt):
                W(TE, tt, Tk(VE))
                i = tt.transpose(pS[par][:, 384:400], lrows[:], sidf[0:16, 0:16])
                D_(TE, i)

            @block.vector
            def _(v):
                W(VE, v, Tk(TE))
                D_(VE, v.tensor_copy(lT[:, c16], pS[par][:, 384:400]))

            @block.scalar
            def _(a):
                W(AE, a, Tk(TE))
                a.activation(nlT[:, c16], pS[par][:, 384:400], AF.Copy, scale=-1.0)
                D_(AE, a.activation(eTc[:, c16], pS[par][:, 384:400], AF.Exp))

        @block.gpsimd
        def _(gq):
            W(GE, gq, Tk(VE), Tk(AE))
            D_(GQ, gq.dma_start(out=hl2[0:1, :], in_=hirows[:]), 16)
            D_(GQ, gq.dma_start(out=hl2[1:2, :], in_=lorows[:]), 16)
            D_(GQ, gq.dma_start(out=lendr[:], in_=lT[127:128, :]), 16)
        hl_tok = Tk(GQ)

        for ck in range(NCH):
            par = ck % 2
            c16 = slice(ck * 16, (ck + 1) * 16)
            cur_pref = prefr[pref_cur % 2]
            nxt_pref = prefr[(pref_cur + 1) % 2]

            @block.tensor
            def _(tt):
                W(TE, tt, hl_tok, Tk(VE))
                tt.matmul(pS[par][:, 400:416], so1f[:], lendr[0:1, c16],
                          start=True, stop=True)
                i = tt.matmul(pS[par][:, 416:432], so1f[:], cur_pref[0:1, :],
                              start=True, stop=True)
                D_(TE, i)

            @block.scalar
            def _(a):
                W(AE, a, Tk(TE), tok_supd[NH - 1])
                D_(AE, a.activation(eendT[:, c16], pS[par][:, 400:416], AF.Exp))

            @block.vector
            def _(v):
                W(VE, v, Tk(TE))
                v.tensor_sub(vsm[:], pS[par][:, 400:416], lT[:, c16])
                i = v.tensor_add(vsm2[:], pS[par][:, 416:432], lT[:, c16])
                D_(VE, i)

            @block.scalar
            def _(a):
                W(AE, a, Tk(VE))
                a.activation(vsm[:], vsm[:], AF.Exp)
                D_(AE, a.activation(egT[:, c16], vsm2[:], AF.Exp))

            @block.vector
            def _(v):
                W(VE, v, Tk(AE), tok_supd[NH - 1])
                v.tensor_mul(endfT[:, c16], vsm[:], dtT[:, c16])
                i = v.tensor_add(nxt_pref[:], cur_pref[:], lendr[0:1, c16])
                D_(VE, i)
            pref_cur += 1

        prep_ve, prep_ae = Tk(VE), Tk(AE)

        @block.sync
        def _(sq):
            W(SE, sq, prep_ve, prep_ae)
            D_(SE, sq.dma_start(out=sp_eg[t], in_=egT[:]), 16)

        for ck in range(NCH):
            gck = t * NCH + ck
            ypar = ck % 2
            Bsl = xbcs[:, 16 * TT + ck * 128:16 * TT + (ck + 1) * 128]
            Csl = xbcs[:, 17 * TT + ck * 128:17 * TT + (ck + 1) * 128]

            if t == 0 and ck == 0:
                toks = {}

            @block.tensor
            def _(tt):
                W(TE, tt, conv_toks[16], prep_ve, prep_ae)
                if gck >= 1:
                    W(TE, tt, btc_tok, toks.get(("xtev", NH - 2)),
                      toks.get(("xtev", NH - 1)))
                i = tt.transpose(pT[ypar][:, 896:1024], Bsl, sid[:])
                D_(TE, i)

            @block.scalar
            def _(a):
                W(AE, a, Tk(TE))
                D_(AE, a.activation(btc[:], pT[ypar][:, 896:1024], AF.Copy))
            btc_tok = Tk(AE)

            prev = toks if gck >= 1 else {}
            toks = {}

            def gtok(key, hh):
                if hh >= 0:
                    return toks.get((key, hh))
                return prev.get((key, hh + NH))

            for h in range(NH):
                p = h % 2
                hsl = slice(h * 128, (h + 1) * 128)
                tok_supd_pre[h] = tok_supd[h]
                dtcol = dtT[:, ck * 16 + h:ck * 16 + h + 1]
                eecol = eendT[:, ck * 16 + h:ck * 16 + h + 1]
                efcol = endfT[:, ck * 16 + h:ck * 16 + h + 1]
                etcol = eTc[:, ck * 16 + h:ck * 16 + h + 1]
                nlcol = nlT[:, ck * 16 + h:ck * 16 + h + 1]

                @block.tensor
                def _(tt):
                    W(TE, tt, conv_toks[h], btc_tok, gtok("xtev", h - 2))
                    i = tt.transpose(pT[p][:, 0:128],
                                     xbcs[:, h * TT + ck * 128:h * TT + (ck + 1) * 128],
                                     sid[:])
                    toks[("trx", h)] = D_(TE, i)
                    W(TE, tt, gtok("scs", h - 2))
                    i = tt.matmul(pS[p][:, 0:128], Bsl, Csl, start=True, stop=True)
                    W(TE, tt, gtok("dex", h - 2), hl_tok)
                    i = tt.matmul(pS[p][:, 128:256], so2[:],
                                  hl2[0:2, h * TT + ck * 128:h * TT + (ck + 1) * 128],
                                  start=True, stop=False)
                    i = tt.matmul(pS[p][:, 128:256], smk[:], sid[:],
                                  start=False, stop=True)
                    toks[("pf", h)] = D_(TE, i)

                @block.scalar
                def _(a):
                    W(AE, a, toks[("trx", h)], gtok("ds", h - 2))
                    i = a.activation(xtc[p][:], pT[p][:, 0:128], AF.Copy)
                    toks[("xtev", h)] = D_(AE, i)
                    W(AE, a, toks[("pf", h)], gtok("scs", h - 2))
                    i = a.activation(sDm[p][:], pS[p][:, 128:256], AF.Exp, bias=nlcol)
                    toks[("dex", h)] = D_(AE, i)

                @block.vector
                def _(v):
                    W(VE, v, toks[("dex", h)], gtok("y1", h - 2))
                    i = v.scalar_tensor_tensor(ssc[p][:], pS[p][:, 0:128], dtcol,
                                               sDm[p][:], ALU.mult, ALU.mult)
                    toks[("scs", h)] = D_(VE, i)

                @block.tensor
                def _(tt):
                    W(TE, tt, toks[("scs", h)], toks[("xtev", h)],
                      gtok("yfin", h - 2))
                    i = tt.matmul(pY[p][:, 0:128], ssc[p][:], xtc[p][:],
                                  start=True, stop=False)
                    i = tt.matmul(pY[p][:, 0:128], sddiag[:, hsl], xtc[p][:],
                                  start=False, stop=True)
                    toks[("y1", h)] = D_(TE, i)
                    W(TE, tt, tok_supd[h], gtok("y2s", h - 2))
                    i = tt.matmul(pY[p][:, 128:256], Csl, sSb[:, hsl],
                                  start=True, stop=True)
                    toks[("y2", h)] = D_(TE, i)

                @block.scalar
                def _(a):
                    W(AE, a, toks[("y2", h)], gtok("yfin", h - 2))
                    i = a.activation(sy2[p][:], pY[p][:, 128:256], AF.Copy, scale=etcol)
                    toks[("y2s", h)] = D_(AE, i)

                @block.vector
                def _(v):
                    W(VE, v, toks[("y2s", h)], tok_yspill.get((gck, h // 8) if h % 8 == 0 else None)
                      if False else toks[("y2s", h)])
                    W(VE, v, tok_yspill.get((gck - 1, h // 8)))
                    hhalf = h // 8
                    i = v.tensor_add(ylc[hhalf][:, (h % 8) * 128:(h % 8 + 1) * 128],
                                     pY[p][:, 0:128], sy2[p][:])
                    toks[("yfin", h)] = D_(VE, i)
                if h == 7 or h == NH - 1:
                    hhalf = h // 8

                    @block.sync
                    def _(sq):
                        W(SE, sq, toks[("yfin", h)])
                        tok_yspill[(gck, hhalf)] = D_(SE, sq.dma_start(
                            out=sp_y[t, ck][:, hhalf * 1024:(hhalf + 1) * 1024],
                            in_=ylc[hhalf][:]), 16)

                @block.vector
                def _(v):
                    W(VE, v, btc_tok, gtok("ds", h - 2))
                    i = v.tensor_scalar(btS[p][:], btc[:], efcol, None, ALU.mult)
                    toks[("bts", h)] = D_(VE, i)

                @block.tensor
                def _(tt):
                    W(TE, tt, toks[("bts", h)], gtok("dsev", h - 2))
                    i = tt.matmul(pS[p][:, 256:384], btS[p][:], xtc[p][:],
                                  start=True, stop=True)
                    toks[("ds", h)] = D_(TE, i)

                @block.scalar
                def _(a):
                    W(AE, a, toks[("ds", h)], tok_supd_pre[h])
                    i = a.activation(sdS[p][:], pS[p][:, 256:384], AF.Copy)
                    toks[("dsev", h)] = D_(AE, i)

                @block.vector
                def _(v):
                    W(VE, v, toks[("dsev", h)], toks[("y2", h)], tok_supd[h])
                    i = v.scalar_tensor_tensor(sS[:, hsl], sS[:, hsl], eecol,
                                               sdS[p][:], ALU.mult, ALU.add)
                    toks[("sup", h)] = D_(VE, i)

                @block.gpsimd
                def _(g):
                    W(GE, g, toks[("sup", h)])
                    i = g.tensor_copy(sSb[:, hsl], sS[:, hsl])
                    tok_supd[h] = D_(GE, i)



    @block.gpsimd
    def _(gq):
        W(GE, gq, tok_supd[NH - 1])
        rreg = gq.value_load(srole[0:1, 0:1])
        D_(GQ, gq.dma_start(out=sh_pay[rreg], in_=sS[:]), 16)
        gq.wait_ge(sy.sem[GQ], sy.n[GQ])
        gq.memset(sflag[:], MAGIC)
        D_(GQ, gq.dma_start(out=sh_flag[rreg], in_=sflag[:]), 16)
        gq.wait_ge(sy.sem[GQ], sy.n[GQ])
        D_(GE, gq.memset(sflag[:], 0))

    p1_final = [Tk(e) for e in Syn.ENGS]

    @block.tensor
    def _(tt):
        W(TE, tt, *p1_final)

    @block.scalar
    def _(a):
        W(AE, a, *p1_final)

    @block.vector
    def _(v):
        W(VE, v, *p1_final)

    @block.gpsimd
    def _(g):
        W(GE, g, *p1_final)

    @block.sync
    def _(sq):
        W(SE, sq, *p1_final)

    p1.close()

    p2 = ExitStack()
    S2 = lambda n, sh, dt: p2.enter_context(nc.sbuf_tensor(n, sh, dt))
    w1b = [S2("w1b0", [128, 8 * 512], BF16), S2("w1b1", [128, 8 * 512], BF16)]
    w2b = S2("w2b", [128, 9 * 4 * 512], BF16)
    wrb = S2("wrb", [128, 8 * 8], BF16)
    xnF2 = S2("xnF2", [128, 8 * TT], BF16)
    szall = S2("szall", [128, NCH * DI], BF16)
    stile = S2("stile", [128, NCH * DM], F32)
    yp2 = [S2("yp20", [128, DI], F32), S2("yp21", [128, DI], F32)]
    ygc = S2("ygc", [128, DI], F32)
    zp2 = [S2("zp20", [128, DI], BF16), S2("zp21", [128, DI], BF16)]
    xnp2 = [S2("xnp20", [128, DM], F32), S2("xnp21", [128, DM], F32)]
    xp2 = [S2("xp20", [128, 512], F32), S2("xp21", [128, 512], F32)]
    Ct2 = S2("Ct2", [128, TT], BF16)
    eg2 = S2("eg2", [128, NCH * 16], F32)
    sSinb = S2("sSinb", [128, NH * HD], BF16)
    y3F = S2("y3F", [128, 16 * 128], BF16)
    y3b = S2("y3b", [128, DI], BF16)
    st2 = S2("st2", [128, 12], F32)
    probc = S2("probc", [128, NCH * 8], F32)
    probb = [S2("probb0", [128, 8], BF16), S2("probb1", [128, 8], BF16)]
    probT = S2("probT", [8, TT], BF16)
    rb2 = [S2("rb20", [128, TT], BF16), S2("rb21", [128, TT], BF16)]
    pb2 = [S2("pb20", [128, TT], BF16), S2("pb21", [128, TT], BF16)]
    gph = [S2("gph0", [128, 512], F32), S2("gph1", [128, 512], F32)]
    ob2 = [S2("ob20", [128, 512], F32), S2("ob21", [128, 512], F32)]
    vs2 = S2("vs2", [128, 16], F32)


    ptmp = ExitStack()
    wtmp = ptmp.enter_context(nc.sbuf_tensor("wtmp", [128, DM], F32))
    wos = ptmp.enter_context(nc.sbuf_tensor("wos", [128, DM], BF16))
    sSin = ptmp.enter_context(nc.sbuf_tensor("sSin", [128, NH * HD], F32))
    wcast_toks = {}
    for ctile in range(16):
        @block.sync
        def _(sq):
            W(SE, sq, wcast_toks.get(ctile - 1))
            D_(SE, sq.dma_start(out=wtmp[:], in_=woutr[:, ctile, :]), 16)

        @block.vector
        def _(v):
            W(VE, v, Tk(SE), wcast_toks.get(ctile - 1))
            D_(VE, v.tensor_scalar(wos[:], wtmp[:],
                                   srms[:, ctile:ctile + 1], None, ALU.mult))

        @block.sync
        def _(sq):
            W(SE, sq, Tk(VE))
            i = sq.dma_start(out=sp_wo[:, ctile * DM:(ctile + 1) * DM], in_=wos[:])
            wcast_toks[ctile] = D_(SE, i, 16)

    w1tok = {}

    @block.gpsimd
    def _(gq):
        D_(G2, gq.dma_start(out=wrb[:], in_=wrr.rearrange("p a b -> p (a b)")), 16)
        i = gq.dma_start(out=w1b[0][:], in_=w1r[0].rearrange("p a b -> p (a b)"))
        w1tok[0] = D_(G2, i, 16)
        gq.wait_ge(sy.sem[GQ], sy.n[GQ])
        gq.memset(sflag[:], 0)
        with gq.register("cnt") as cnt, gq.register("fv") as fv, gq.register("ctr") as ctr:
            gq.reg_mov(cnt, sy.n[GQ])
            gq.reg_mov(fv, 0)
            def cond():
                gq.reg_alu(ctr, fv, MAGIC, ALU.bitwise_xor)
                return ctr
            with gq.While(cond):
                gq.reg_add(cnt, cnt, 16)
                gq.dma_start(out=sflag[:], in_=sh_flag[0]).then_inc(sy.sem[GQ], 16)
                gq.wait_ge(sy.sem[GQ], cnt)
                gq.reg_mov(fv, gq.value_load(sflag[0:1, 0:1]))
            gq.reg_add(cnt, cnt, 16)
            gq.dma_start(out=sSin[:], in_=sh_pay[0]).then_inc(sy.sem[GQ], 16)
            gq.wait_ge(sy.sem[GQ], cnt)
        g = gq.engine_nop()
        D_(GE, g)
    sin_ready = Tk(GE)

    @block.vector
    def _(v):
        W(VE, v, sin_ready)
        D_(VE, v.tensor_scalar(sSin[:], sSin[:], sgate[:, 0:1], None, ALU.mult))
        selfwait(v)
        i = v.tensor_copy(sSinb[:], sSin[:])
        D_(VE, i)
    sin_tok = Tk(VE)
    ptmp.close()
    hball = S2("hball", [128, 9 * 4 * TT], BF16)

    tok_p2load = {}
    tok_p2cons = {}
    tok_tilecons = {}
    tok_ospill = {}
    y3ev_prev = None
    sev_prev = None
    prev_sz2 = prev_sz3 = None
    gate_toks = {}
    sev_toks = {}
    gprev_carry = {}
    osprev_carry = {}
    pt_prev = None
    g_all_prev = None
    rt_used_tok = None
    w2_free_tok = None

    for t in range(ntile):
        @block.sync
        def _(sq):
            W(SE, sq, tok_tilecons.get(t - 1))
            D_(SE, sq.dma_start(out=Ct2[:], in_=sp_C[:, t * TT:(t + 1) * TT]), 16)
            D_(SE, sq.dma_start(out=eg2[:], in_=sp_eg[t]), 16)
            D_(SE, sq.dma_start(out=xnF2[:], in_=sp_xnF[:, :, t * TT:(t + 1) * TT]), 16)
        tload_tok = Tk(SE)


        @block.gpsimd
        def _(gq):
            W(GE, gq, w2_free_tok)
            if t == 0:
                W(GE, gq, wcast_toks[15])
            i = gq.dma_start(out=w2b[:, 0:16 * DM], in_=sp_wo)
            D_(G2, i, 16)
        wol_tok = Tk(G2)

        warm(AF.Silu)
        sz_toks = []
        zl_toks = []
        for ck in range(NCH):
            @block.sync
            def _(sq):
                if ck >= 2:
                    W(SE, sq, sz_toks[ck - 2])
                elif t >= 1:
                    W(SE, sq, prev_sz2, prev_sz3)
                i = sq.dma_start(out=zp2[ck % 2][:], in_=sp_z[t, ck])
                zl_toks.append(D_(SE, i, 16))

            @block.scalar
            def _(a):
                W(AE, a, zl_toks[ck], Tk(SE))
                i = a.activation(szall[:, ck * DI:(ck + 1) * DI], zp2[ck % 2][:], AF.Silu)
                sz_toks.append(D_(AE, i))

            if bar3:
                hardbar()
            if dbg:
                @block.sync
                def _(sq):
                    W(SE, sq, sz_toks[ck])
                    D_(SE, sq.dma_start(out=sp_zp[t, ck], in_=zp2[ck % 2][:]), 16)
        prev_sz2, prev_sz3 = sz_toks[2], sz_toks[3]

        warm(AF.Ln)
        for ck in range(NCH):
            gck = t * NCH + ck
            cpar = ck % 2
            ctoks = {}

            @block.sync
            def _(sq):
                W(SE, sq, sev_toks.get(gck - 2))
                D_(SE, sq.dma_start(out=yp2[cpar][:], in_=sp_y[t, ck]), 16)
                i = sq.dma_start(out=xnp2[cpar][:], in_=sp_xnT[t, ck])
                tok_p2load[gck] = D_(SE, i, 16)
            quiesce_se = Tk(SE)

            for h in range(NH):
                p = h % 2
                hsl = slice(h * 128, (h + 1) * 128)

                @block.tensor
                def _(tt):
                    W(TE, tt, sin_tok, tload_tok, ctoks.get(("cm", h - 2)))
                    if h < 2:
                        W(TE, tt, sev_prev)
                    i = tt.matmul(pS[p][:, 0:128],
                                  Ct2[:, ck * 128:(ck + 1) * 128], sSinb[:, hsl],
                                  start=True, stop=True)
                    ctoks[("mm", h)] = D_(TE, i)

                @block.vector
                def _(v):
                    W(VE, v, ctoks[("mm", h)], tok_p2load[gck],
                      quiesce_se if h == 0 else None)
                    i = v.scalar_tensor_tensor(ygc[:, hsl], pS[p][:, 0:128],
                                               eg2[:, ck * 16 + h:ck * 16 + h + 1],
                                               yp2[cpar][:, hsl], ALU.mult, ALU.add)
                    ctoks[("cm", h)] = D_(VE, i)

            if bar3:
                hardbar()
            if dbg:
                @block.sync
                def _(sq):
                    W(SE, sq, ctoks[("cm", NH - 1)], sz_toks[ck])
                    D_(SE, sq.dma_start(out=sp_yc[t, ck], in_=ygc[:]), 16)
                    D_(SE, sq.dma_start(out=sp_sz[t, ck],
                                        in_=szall[:, ck * DI:(ck + 1) * DI]), 16)

            @block.vector
            def _(v):
                W(VE, v, sz_toks[ck])
                selfwait(v)
                if dbg:
                    W(VE, v, Tk(SE))
                i = v.tensor_mul(yp2[cpar][:], ygc[:],
                                 szall[:, ck * DI:(ck + 1) * DI])
                gate_toks[gck] = D_(VE, i)

            if bar3:
                hardbar()

            @block.vector
            def _(v):
                selfwait(v)
                D_(VE, v.tensor_mul(ygc[:], yp2[cpar][:], yp2[cpar][:]))
                selfwait(v)
                D_(VE, v.reduce_sum(st2[:, 0:1], ygc[:], axis=AX))
                selfwait(v)
                D_(VE, v.tensor_scalar(st2[:, 1:2], st2[:, 0:1], 1.0 / DI, None, ALU.mult))

            @block.scalar
            def _(a):
                W(AE, a, Tk(VE))
                D_(AE, a.activation(st2[:, 2:3], st2[:, 1:2], AF.Ln, bias=EPS))
                selfwait(a, AE)
                D_(AE, a.activation(st2[:, 3:4], st2[:, 2:3], AF.Exp, scale=-0.5))

            @block.vector
            def _(v):
                W(VE, v, Tk(AE))
                selfwait(v)
                D_(VE, v.tensor_scalar(y3b[:], yp2[cpar][:], st2[:, 3:4], None, ALU.mult))
            y3_tok = Tk(VE)

            if bar3:
                hardbar()
            for g2i in range(2):
                @block.tensor
                def _(tt):
                    W(TE, tt, y3_tok, y3ev_prev, pt_prev)
                    i = None
                    for d in range(8):
                        cti = g2i * 8 + d
                        i = tt.transpose(pT[g2i][:, d * 128:(d + 1) * 128],
                                         y3b[:, cti * 128:(cti + 1) * 128], sid[:])
                    ctoks[("y3t", g2i)] = D_(TE, i)

                @block.scalar
                def _(a):
                    W(AE, a, ctoks[("y3t", g2i)])
                    i = None
                    for d in range(8):
                        cti = g2i * 8 + d
                        i = a.activation(y3F[:, cti * 128:(cti + 1) * 128],
                                         pT[g2i][:, d * 128:(d + 1) * 128], AF.Copy)
                    ctoks[("y3ev", g2i)] = D_(AE, i)
            y3ev_prev = ctoks[("y3ev", 1)]

            for half in range(2):
                @block.tensor
                def _(tt):
                    W(TE, tt, ctoks[("y3ev", 1)], wol_tok)
                    i = None
                    for cti in range(16):
                        i = tt.matmul(pS[half][:, 0:512],
                                      y3F[:, cti * 128:(cti + 1) * 128],
                                      w2b[:, cti * DM + half * 512:cti * DM + (half + 1) * 512],
                                      start=(cti == 0), stop=(cti == 15))
                    ctoks[("m2", half)] = D_(TE, i)

                @block.vector
                def _(v):
                    W(VE, v, ctoks[("m2", half)])
                    i = v.tensor_add(
                        stile[:, ck * DM + half * 512:ck * DM + (half + 1) * 512],
                        pS[half][:, 0:512],
                        xnp2[cpar][:, half * 512:(half + 1) * 512])
                    ctoks[("sev", half)] = D_(VE, i)
            sev_prev = ctoks[("sev", 1)]
            sev_toks[gck] = sev_prev
            if bar2:
                _all = [Tk(e) for e in Syn.ENGS if e not in ("gq",)]
                @block.tensor
                def _(tt):
                    W(TE, tt, *_all)
                @block.scalar
                def _(a):
                    W(AE, a, *_all)
                @block.vector
                def _(v):
                    W(VE, v, *_all)
                @block.gpsimd
                def _(g):
                    W(GE, g, *_all)
                @block.sync
                def _(sq):
                    W(SE, sq, *_all)
            if dbg:
                @block.sync
                def _(sq):
                    W(SE, sq, sev_prev, ctoks[("sev", 0)])
                    D_(SE, sq.dma_start(out=sp_s[t, ck],
                                        in_=stile[:, ck * DM:(ck + 1) * DM]), 16)
                    D_(SE, sq.dma_start(out=sp_x2[t, ck], in_=xnp2[cpar][:]), 16)
                    D_(SE, sq.dma_start(out=sp_r[t, ck], in_=st2[:]), 16)
                    D_(SE, sq.dma_start(out=sp_yg[t, ck], in_=yp2[cpar][:]), 16)
                    D_(SE, sq.dma_start(out=sp_y3[t, ck], in_=y3F[:]), 16)

            if bar3:
                hardbar()

            @block.tensor
            def _(tt):
                W(TE, tt, tload_tok, w1tok[0], rt_used_tok, g_all_prev)
                i = None
                for d in range(8):
                    i = tt.matmul(pY[0][:, 0:8],
                                  xnF2[:, d * TT + ck * 128:d * TT + (ck + 1) * 128],
                                  wrb[:, d * 8:(d + 1) * 8], start=(d == 0), stop=(d == 7))
                D_(TE, i)

            @block.vector
            def _(v):
                W(VE, v, Tk(TE), Tk(AE))
                lg = probc[:, ck * 8:(ck + 1) * 8]
                D_(VE, v.tensor_copy(lg, pY[0][:, 0:8]))
                selfwait(v)
                D_(VE, v.reduce_max(st2[:, 4:5], lg, axis=AX))
                selfwait(v)
                D_(VE, v.tensor_scalar(vs2[:, 0:8], lg, st2[:, 4:5], None, ALU.is_ge))
                selfwait(v)
                D_(VE, v.scalar_tensor_tensor(vs2[:, 8:16], vs2[:, 0:8], -BIG, lg,
                                              ALU.mult, ALU.add))
                selfwait(v)
                D_(VE, v.reduce_max(st2[:, 5:6], vs2[:, 8:16], axis=AX))
                selfwait(v)
                i = v.tensor_scalar(st2[:, 6:7], st2[:, 5:6], st2[:, 4:5], None,
                                    ALU.subtract)
                D_(VE, i)
            rt_used_tok = Tk(VE)

            @block.scalar
            def _(a):
                W(AE, a, Tk(VE))
                D_(AE, a.activation(st2[:, 7:8], st2[:, 6:7], AF.Exp))

            @block.vector
            def _(v):
                W(VE, v, Tk(AE))
                D_(VE, v.tensor_scalar(st2[:, 8:9], st2[:, 7:8], 1.0, None, ALU.add))
                selfwait(v)
                D_(VE, v.reciprocal(st2[:, 9:10], st2[:, 8:9]))
                selfwait(v)
                D_(VE, v.tensor_scalar(st2[:, 10:11], st2[:, 9:10], 1.0, None, ALU.is_ge))
                selfwait(v)
                i = v.tensor_scalar(probc[:, ck * 8:(ck + 1) * 8],
                                    vs2[:, 0:8], st2[:, 10:11], None, ALU.mult)
                D_(VE, i)

        s_done_tok = sev_prev

        pt_toks = {}
        for ck in range(NCH):
            @block.vector
            def _(v):
                selfwait(v)
                D_(VE, v.tensor_copy(probb[ck % 2][:], probc[:, ck * 8:(ck + 1) * 8]))

            @block.tensor
            def _(tt):
                W(TE, tt, Tk(VE), y3ev_prev, pt_toks.get(ck - 2))
                D_(TE, tt.transpose(pT[ck % 2][0:8, 0:128], probb[ck % 2][:], sid[:]))

            @block.scalar
            def _(a):
                W(AE, a, Tk(TE))
                i = a.activation(probT[:, ck * 128:(ck + 1) * 128],
                                 pT[ck % 2][0:8, 0:128], AF.Copy)
                pt_toks[ck] = D_(AE, i)
        pt_prev = pt_toks[NCH - 1]

        et = {}
        for e in range(9):
            if e < 8:
                @block.gpsimd
                def _(gq):
                    W(GE, gq, et.get(("hmm3", e - 1)))
                    i = gq.dma_start(out=w1b[(e + 1) % 2][:],
                                     in_=w1r[e + 1].rearrange("p a b -> p (a b)"))
                    w1tok[e + 1] = D_(G2, i, 16)

                @block.tensor
                def _(tt):
                    W(TE, tt, pt_toks[NCH - 1], et.get(("pbev", e - 2)))
                    i = tt.matmul(pY[1][:, 0:512], ssel[:, e * 128:(e + 1) * 128],
                                  probT[:], start=True, stop=True)
                    et[("pbmm", e)] = D_(TE, i)

                @block.scalar
                def _(a):
                    W(AE, a, et[("pbmm", e)], et.get(("hb", (e - 2, 3))))
                    i = a.activation(pb2[e % 2][:], pY[1][:, 0:512], AF.Copy)
                    et[("pbev", e)] = D_(AE, i)

            w1cur = w1b[e % 2]
            for ft in range(4):
                pz = [pA, pB][ft % 2]

                @block.tensor
                def _(tt):
                    W(TE, tt, w1tok[e], tload_tok, et.get(("sqev", (e, ft - 2))))
                    if ft == 0:
                        W(TE, tt, et.get(("sqev", (e - 1, 2))),
                          et.get(("sqev", (e - 1, 3))))
                    i = None
                    for d in range(8):
                        i = tt.matmul(pz[:, 0:512],
                                      w1cur[:, d * 512 + ft * 128:d * 512 + (ft + 1) * 128],
                                      xnF2[:, d * TT:(d + 1) * TT],
                                      start=(d == 0), stop=(d == 7))
                    et[("hmm", (e, ft))] = D_(TE, i)
                    if ft == 3:
                        et[("hmm3", e)] = et[("hmm", (e, ft))]

                @block.scalar
                def _(a):
                    W(AE, a, et[("hmm", (e, ft))],
                      et.get(("hb", (e, ft - 2))) if ft >= 2
                      else et.get(("hb", (e - 1, ft + 2))))
                    D_(AE, a.activation(rb2[ft % 2][:], pz[:, 0:512], AF.Relu))
                    selfwait(a, AE)
                    i = a.activation(rb2[ft % 2][:], rb2[ft % 2][:], AF.Square)
                    et[("sqev", (e, ft))] = D_(AE, i)

                @block.vector
                def _(v):
                    W(VE, v, et[("sqev", (e, ft))])
                    hsl2 = slice((e * 4 + ft) * TT, (e * 4 + ft + 1) * TT)
                    if e < 8:
                        W(VE, v, et[("pbev", e)])
                        i = v.tensor_mul(hball[:, hsl2], rb2[ft % 2][:], pb2[e % 2][:])
                    else:
                        i = v.tensor_copy(hball[:, hsl2], rb2[ft % 2][:])
                    et[("hb", (e, ft))] = D_(VE, i)

        hb_tok = et[("hb", (8, 3))]

        g_toks = {}

        def gseq(half, ck, back):
            s = half * NCH + ck - back
            if s >= 0:
                return g_toks.get(("o", s // NCH, s % NCH))
            return gprev_carry.get(8 + s)

        def ossq(half, ck, back):
            s = half * NCH + ck - back
            if s >= 0:
                return tok_ospill.get(s)
            return osprev_carry.get(8 + s)

        warm(AF.Silu)
        for half in range(2):
            @block.gpsimd
            def _(gq):
                if half == 0:
                    W(GE, gq, sev_prev)
                else:
                    W(GE, gq, g_toks[("mm", 0, NCH - 1)])
                i = gq.dma_start(out=w2b[:],
                                 in_=w2r[half].rearrange("p a b c -> p (a b c)"))
                D_(G2, i, 16)
            w2l_tok = Tk(G2)

            for ck in range(NCH):
                gpar = ck % 2
                gck = t * NCH + ck

                @block.sync
                def _(sq):
                    W(SE, sq, gseq(half, ck, 2))
                    r0 = 3 + t * TT + ck * 128
                    i = sq.dma_start(out=xp2[gpar][:],
                                     in_=xs[r0:r0 + 128, half * 512:(half + 1) * 512])
                    g_toks[("xl", half, ck)] = D_(SE, i, 16)

                @block.tensor
                def _(tt):
                    W(TE, tt, hb_tok, w2l_tok, g_toks.get(("mm", half, ck - 2)))
                    if half == 0 and ck < 2:
                        W(TE, tt, g_all_prev)
                    i = None
                    for ei in range(9):
                        for ft in range(4):
                            i = tt.matmul(pY[gpar][:, 0:512],
                                          hball[:, (ei * 4 + ft) * TT + ck * 128:
                                                (ei * 4 + ft) * TT + (ck + 1) * 128],
                                          w2b[:, (ei * 4 + ft) * 512:(ei * 4 + ft + 1) * 512],
                                          start=(ei == 0 and ft == 0),
                                          stop=(ei == 8 and ft == 3))
                    g_toks[("mm", half, ck)] = D_(TE, i)

                @block.scalar
                def _(a):
                    W(AE, a, g_toks[("mm", half, ck)], gseq(half, ck, 2))
                    D_(AE, a.activation(gph[gpar][:], pY[gpar][:, 0:512], AF.Silu))

                if dbg:
                    @block.sync
                    def _(sq):
                        W(SE, sq, Tk(AE))
                        D_(SE, sq.dma_start(out=sp_g[t, half, ck], in_=gph[gpar][:]), 16)

                @block.vector
                def _(v):
                    W(VE, v, Tk(AE), s_done_tok, g_toks[("xl", half, ck)],
                      ossq(half, ck, 2))
                    if dbg:
                        W(VE, v, Tk(SE))
                    D_(VE, v.tensor_mul(
                        ob2[gpar][:],
                        stile[:, ck * DM + half * 512:ck * DM + (half + 1) * 512],
                        gph[gpar][:]))
                    selfwait(v)
                    i = v.tensor_add(ob2[gpar][:], ob2[gpar][:], xp2[gpar][:])
                    g_toks[("o", half, ck)] = D_(VE, i)

                if bar3:
                    hardbar()

                @block.sync
                def _(sq):
                    W(SE, sq, g_toks[("o", half, ck)])
                    i = sq.dma_start(
                        out=out_h[t * TT + ck * 128:t * TT + (ck + 1) * 128,
                                  half * 512:(half + 1) * 512],
                        in_=ob2[gpar][:])
                    tok_ospill[half * NCH + ck] = D_(SE, i, 16)

        gprev_carry = {s: g_toks.get(("o", s // NCH, s % NCH)) for s in range(8)}
        osprev_carry = {s: tok_ospill.get(s) for s in range(8)}
        g_all_prev = g_toks[("o", 1, NCH - 1)]
        w2_free_tok = g_toks[("mm", 1, NCH - 1)]
        tok_tilecons[t] = g_all_prev
        for ck in range(NCH):
            tok_p2cons[t * NCH + ck] = g_toks[("o", 1, ck)]

    @block.sync
    def _(sq):
        sq.wait_ge(sy.sem[SE], sy.n[SE])

    blkctx.__exit__(None, None, None)
    p2.close()
    ctx.close()
    return nc


_PROG_CACHE = {}


def _get_prog(ntile):
    if ntile not in _PROG_CACHE:
        _PROG_CACHE[ntile] = build(ntile, bar3=True)
    return _PROG_CACHE[ntile]


def make_inputs(inputs, ntile=8):
    f = np.float32
    x = np.ascontiguousarray(np.asarray(inputs["L0_attn_x"], f))
    Bt, L, _ = x.shape
    half = L // 2
    tcore = ntile * TT

    W_in = np.asarray(inputs["L0_attn_W_in"], f)
    winr = np.ascontiguousarray(W_in.reshape(8, 128, DPROJ).transpose(1, 0, 2))
    W_out = np.asarray(inputs["L0_attn_W_out"], f)
    woutr = np.ascontiguousarray(W_out.reshape(16, 128, DM).transpose(1, 0, 2))
    W1 = np.asarray(inputs["L0_ffn_W1"], f)
    Ws1 = np.asarray(inputs["L0_ffn_Ws1"], f)
    W1cat = np.concatenate([W1, Ws1[None]], 0)
    w1r = np.ascontiguousarray(W1cat.reshape(9, 8, 128, 512).transpose(0, 2, 1, 3))
    W2 = np.asarray(inputs["L0_ffn_W2"], f)
    Ws2 = np.asarray(inputs["L0_ffn_Ws2"], f)
    W2cat = np.concatenate([W2, Ws2[None]], 0)
    w2r = np.ascontiguousarray(
        W2cat.reshape(9, 4, 128, 2, 512).transpose(3, 2, 0, 1, 4))
    Wr = np.asarray(inputs["L0_ffn_Wr"], f)
    wrr = np.ascontiguousarray(Wr.reshape(8, 128, 8).transpose(1, 0, 2))
    convw = np.ascontiguousarray(
        np.asarray(inputs["L0_attn_conv_w"], f).reshape(NCT, 128, DCONV).transpose(1, 0, 2))
    convb = np.ascontiguousarray(
        np.asarray(inputs["L0_attn_conv_b"], f).reshape(NCT, 128).T)
    lnwc = np.ascontiguousarray(np.asarray(inputs["L0_attn_ln_w"], f).reshape(8, 128).T)
    lnbc = np.ascontiguousarray(np.asarray(inputs["L0_attn_ln_b"], f).reshape(8, 128).T)
    rmscc = np.ascontiguousarray(np.asarray(inputs["L0_attn_rms_w"], f).reshape(16, 128).T)
    alogc = np.asarray(inputs["L0_attn_A_log"], f).reshape(16, 1)
    dtbbc = np.ascontiguousarray(
        np.broadcast_to(np.asarray(inputs["L0_attn_dt_bias"], f), (128, 16)))
    Dv = np.asarray(inputs["L0_attn_D"], f)
    ddiagf = np.zeros((128, 16 * 128), f)
    for h in range(16):
        np.fill_diagonal(ddiagf[:, h * 128:(h + 1) * 128], Dv[h])

    identb = np.eye(128, dtype=bf16)
    identf = np.eye(128, dtype=f)
    ones2b = np.ones((2, 128), dtype=bf16)
    ones1f = np.ones((1, 128), f)
    ones16f = np.ones((16, 128), f)
    maskTb = np.where(np.arange(128)[None, :] > np.arange(128)[:, None],
                      np.float32(-BIG), np.float32(0)).astype(bf16)
    selbv = np.zeros((8, 8 * 128), dtype=bf16)
    for e in range(8):
        selbv[e, e * 128:(e + 1) * 128] = bf16(1.0)

    fcstv = np.broadcast_to(np.array([EPS, 1.0, -1.0, -0.5, 0.5, 0.0], f), (128, 6))
    fcstv = np.ascontiguousarray(fcstv)
    common = dict(fcst=fcstv, winr=winr, woutr=woutr, w1r=w1r, w2r=w2r, wrr=wrr,
                  convw=convw, convb=convb, lnw=lnwc, lnb=lnbc, rmsc=rmscc,
                  alog=alogc, dtbb=dtbbc, ddiagf=ddiagf, identb=identb,
                  identf=identf, ones2b=ones2b, ones1f=ones1f, ones16f=ones16f,
                  maskTb=maskTb, selb=selbv)

    in_maps = []
    for k in range(8):
        b, hf = (k >> 1) % Bt, k & 1
        start = hf * half
        xsk = np.zeros((3 + tcore, DM), f)
        xsk[3:] = x[b, start:start + tcore]
        if hf == 1:
            xsk[0:3] = x[b, start - 3:start]
        m = dict(common)
        m["xs"] = xsk
        m["gates"] = np.full((128, 1), float(hf), f)
        m["role"] = np.array([[hf, 0]], np.int32)
        in_maps.append(m)
    return in_maps


def kernel(**inputs):
    ntile = 8
    nc = _get_prog(ntile)
    in_maps = make_inputs(inputs, ntile)
    res = run_bass_kernel_spmd(nc, in_maps, core_ids=list(range(8)))
    x = np.asarray(inputs["L0_attn_x"], np.float32)
    Bt, L, _ = x.shape
    out = np.empty_like(x)
    half = L // 2
    for k in range(8):
        b, hf = k >> 1, k & 1
        out[b, hf * half:(hf + 1) * half] = res.results[k]["out"]
    return out


if __name__ == "__main__":
    nc = build(1)
    print("built ok; sbuf bytes remaining:", nc.sbuf_bytes_remaining)
